# version 05
# baseline (speedup 1.0000x reference)
import numpy as np
import concourse.bass as bass
import concourse.mybir as mybir
from concourse.bass_utils import run_bass_kernel_spmd

F32 = mybir.dt.float32
I8 = mybir.dt.int8
AF = mybir.ActivationFunctionType
OP = mybir.AluOpType
AX = mybir.AxisListType

D = 1024
DV = 512
NH = 8
DH = 64
I_FFN = 2816
NBLK_D = D // 16
NBLK_I = I_FFN // 16


class Sem:
    def __init__(self, h):
        self.h = h
        self.v = 0

    def next(self, n=1):
        self.v += n
        return self.v


class RT:

    def __init__(self, h):
        self.h = h
        self.c = [0]

    def __call__(self, e, inst):
        self.c[0] += 1
        inst.then_inc(self.h, 1)
        e.wait_ge(self.h, self.c[0])


def gap(h, p0, pn, eoff, dims):
    fs = 1
    for s in h.shape[1:]:
        fs *= int(s)
    return bass.AP(h, p0 * fs + eoff, [[fs, pn]] + [list(d) for d in dims])


def emit(rec, fn, waits=(), inc=None, n=1):
    tgt = inc.next(n) if inc is not None else None
    def go(e, fn=fn, waits=tuple(waits), inc=inc, n=n):
        for (s, v) in waits:
            e.wait_ge(s.h, v)
        r = fn(e)
        if inc is not None:
            r.then_inc(inc.h, n)
    rec.append(go)
    return tgt


def run_block(nc, recs):
    with nc.Block() as blk:
        if recs.get("sync"):
            @blk.sync
            def _(e, items=tuple(recs["sync"])):
                for f in items:
                    f(e)
        if recs.get("tensor"):
            @blk.tensor
            def _(e, items=tuple(recs["tensor"])):
                for f in items:
                    f(e)
        if recs.get("vector"):
            @blk.vector
            def _(e, items=tuple(recs["vector"])):
                for f in items:
                    f(e)
        if recs.get("scalar"):
            @blk.scalar
            def _(e, items=tuple(recs["scalar"])):
                for f in items:
                    f(e)
        if recs.get("gpsimd"):
            @blk.gpsimd
            def _(e, items=tuple(recs["gpsimd"])):
                for f in items:
                    f(e)


def new_recs():
    return {"sync": [], "tensor": [], "vector": [], "scalar": [], "gpsimd": []}


def emit_quant(recv, waits, inc, src, N, nblk, tmps, r6col, rt,
               q_dst=None, g2s_dst=None, s2n_dst=None):
    a, b1, b2, b3, b4 = tmps["a"], tmps["b1"], tmps["b2"], tmps["b3"], tmps["b4"]
    bm, sc, rr = tmps["bm"], tmps["sc"], tmps["rr"]
    src_h, src_p0, src_off = src
    x = gap(src_h, src_p0, 128, src_off, [(1, N)])
    xg = gap(src_h, src_p0, 128, src_off, [(16, nblk), (1, 16)])

    def f0(e):
        return e.tensor_reduce(bm[:, 0:nblk], xg, AX.X, OP.max,
                               apply_absolute_value=True)

    v0 = emit(recv, f0, waits=waits, inc=inc)

    def f(e):
        rt(e, e.tensor_scalar(sc[:, 0:nblk], bm[:, 0:nblk], r6col, 1e-12,
                              OP.mult, OP.max))
        rt(e, e.reciprocal(rr[:, 0:nblk], sc[:, 0:nblk]))
        if s2n_dst is not None:
            e.tensor_scalar(s2n_dst, sc[:, 0:nblk], 0.5, None, OP.mult)
        else:
            rt(e, e.tensor_scalar(bm[:, 0:nblk], sc[:, 0:nblk], 0.5, None,
                                  OP.mult))
        rb = gap(rr, 0, 128, 0, [(1, nblk), (0, 16)])
        rt(e, e.tensor_scalar(b1[:, 0:N], x, 0.0, None, OP.is_lt))
        rt(e, e.scalar_tensor_tensor(b2[:, 0:N], x, 0.0, b1[:, 0:N],
                                     OP.is_gt, OP.subtract))
        rt(e, e.tensor_tensor(a[:, 0:N], x, rb, OP.mult))
        rt(e, e.tensor_tensor(a[:, 0:N], a[:, 0:N], b2[:, 0:N], OP.mult))
        rt(e, e.tensor_scalar(b1[:, 0:N], a[:, 0:N], 0.25, None, OP.is_gt))
        rt(e, e.scalar_tensor_tensor(b3[:, 0:N], a[:, 0:N], 0.75, b1[:, 0:N],
                                     OP.is_gt, OP.add))
        rt(e, e.scalar_tensor_tensor(b1[:, 0:N], a[:, 0:N], 1.25, b3[:, 0:N],
                                     OP.is_gt, OP.add))
        rt(e, e.scalar_tensor_tensor(b3[:, 0:N], a[:, 0:N], 1.75, b1[:, 0:N],
                                     OP.is_gt, OP.add))
        rt(e, e.tensor_scalar(b1[:, 0:N], a[:, 0:N], 5.0, 2.0, OP.is_gt,
                              OP.mult))
        rt(e, e.scalar_tensor_tensor(b4[:, 0:N], a[:, 0:N], 2.5, b1[:, 0:N],
                                     OP.is_gt, OP.add))
        rt(e, e.scalar_tensor_tensor(b1[:, 0:N], a[:, 0:N], 3.5, b4[:, 0:N],
                                     OP.is_gt, OP.add))
        rt(e, e.scalar_tensor_tensor(b4[:, 0:N], b1[:, 0:N], 2.0, b3[:, 0:N],
                                     OP.mult, OP.add))
        if g2s_dst is not None:
            inst = e.tensor_tensor(g2s_dst, b4[:, 0:N], b2[:, 0:N], OP.mult)
        else:
            rt(e, e.tensor_tensor(b1[:, 0:N], b4[:, 0:N], b2[:, 0:N], OP.mult))
            s2b = gap(bm, 0, 128, 0, [(1, nblk), (0, 16)])
            inst = e.tensor_tensor(q_dst, b1[:, 0:N], s2b, OP.mult)
        return inst

    return emit(recv, f, waits=[(inc, v0)], inc=inc)


def build(NB=2, S=2048, dbg=False):
    T = NB * S
    NT = T // 128
    NTS = S // 128
    NIC = S // 512
    assert S % 512 == 0
    IC_LIST = []
    off = 0
    while off < I_FFN:
        sz = min(512, I_FFN - off)
        IC_LIST.append((off, sz))
        off += sz

    nc = bass.Bass("TRN2", target_bir_lowering=False, detect_race_conditions=False)

    x_h = nc.dram_tensor("x", [T, D], F32, kind="ExternalInput")
    wqk_h = nc.dram_tensor("wqk", [D, D], F32, kind="ExternalInput")
    bqk_h = nc.dram_tensor("bqk", [D], F32, kind="ExternalInput")
    wv_h = nc.dram_tensor("wv", [D, DV], F32, kind="ExternalInput")
    bv_h = nc.dram_tensor("bv", [DV], F32, kind="ExternalInput")
    wo_h = nc.dram_tensor("wo", [DV, D], F32, kind="ExternalInput")
    bo_h = nc.dram_tensor("bo", [D], F32, kind="ExternalInput")
    gam_h = nc.dram_tensor("gam", [D], F32, kind="ExternalInput")
    bet_h = nc.dram_tensor("bet", [D], F32, kind="ExternalInput")
    wg_h = nc.dram_tensor("wg", [I_FFN, D], F32, kind="ExternalInput")
    wu_h = nc.dram_tensor("wu", [I_FFN, D], F32, kind="ExternalInput")
    wd_h = nc.dram_tensor("wd", [D, I_FFN], F32, kind="ExternalInput")
    id_h = nc.dram_tensor("ident", [128, 128], F32, kind="ExternalInput")
    out_h = nc.dram_tensor("out", [T, D], F32, kind="ExternalOutput")
    kd = "ExternalOutput" if dbg else "Internal"
    wqtg_h = nc.dram_tensor("wqtg", [D, I_FFN], F32, kind=kd)
    wqtu_h = nc.dram_tensor("wqtu", [D, I_FFN], F32, kind=kd)
    wdt_h = nc.dram_tensor("wdt", [I_FFN, D], F32, kind=kd)
    g2h_h = nc.dram_tensor("g2h", [T, I_FFN], I8, kind=kd)
    sch_h = nc.dram_tensor("sch", [T, NBLK_I], F32, kind=kd)
    osc_h = nc.dram_tensor("osc", [DV, T], F32, kind=kd)
    if dbg:
        qkdbg_h = [nc.dram_tensor(f"qkdbg{b}", [128, 8, S], F32, kind=kd)
                   for b in range(NB)]
        m8dbg_h = [nc.dram_tensor(f"m8dbg{b}", [128, 12], F32, kind=kd)
                   for b in range(NB)]
        xqdbg_h = nc.dram_tensor("xqdbg", [T, D], F32, kind=kd)
        hdbg_h = nc.dram_tensor("hdbg", [T, I_FFN], F32, kind=kd)

    ctx = []

    def alloc(cm):
        h = cm.__enter__()
        ctx.append(cm)
        return h
    scopes = []

    def begin_scope():
        scopes.append([])

    def scp(cm):
        h = cm.__enter__()
        scopes[-1].append(cm)
        return h

    def end_scope():
        for cm in reversed(scopes.pop()):
            cm.__exit__(None, None, None)

    ident = alloc(nc.sbuf_tensor("identsb", [128, 128], F32))
    bqs = alloc(nc.sbuf_tensor("bqs", [128, 8], F32))
    bqk8 = alloc(nc.sbuf_tensor("bqk8", [128, 8], F32))
    bvrep = alloc(nc.sbuf_tensor("bvrep", [128, DV], F32))
    brow = alloc(nc.sbuf_tensor("brow", [1, D], F32))
    gamc = alloc(nc.sbuf_tensor("gamc", [128, D], F32))
    betc = alloc(nc.sbuf_tensor("betc", [128, D], F32))
    mu = alloc(nc.sbuf_tensor("mu", [128, NT], F32))
    rstd = alloc(nc.sbuf_tensor("rstd", [128, NT], F32))
    sxb = alloc(nc.sbuf_tensor("sxb", [128, NT], F32))
    sqb = alloc(nc.sbuf_tensor("sqb", [128, NT], F32))
    r6 = alloc(nc.sbuf_tensor("r6", [128, 1], F32))
    epsc = alloc(nc.sbuf_tensor("epsc", [128, 1], F32))
    cc = alloc(nc.sbuf_tensor("cc", [128, 1], F32))
    tA = alloc(nc.sbuf_tensor("tA", [128, 1], F32))
    tB = alloc(nc.sbuf_tensor("tB", [128, 1], F32))
    pat = alloc(nc.sbuf_tensor("pat", [128, 4, NH], F32))
    onesr = alloc(nc.sbuf_tensor("onesr", [1, 128], F32))
    qt_bm = alloc(nc.sbuf_tensor("qt_bm", [128, NBLK_I], F32))
    qt_sc = alloc(nc.sbuf_tensor("qt_sc", [128, NBLK_I], F32))
    qt_rr = alloc(nc.sbuf_tensor("qt_rr", [128, NBLK_I], F32))
    xb = [alloc(nc.sbuf_tensor(f"xb{i}", [128, D], F32)) for i in range(2)]
    xnb = alloc(nc.sbuf_tensor("xnb", [128, D], F32))
    dump = alloc(nc.sbuf_tensor("dump", [128, D], F32))

    rtd = RT(alloc(nc.semaphore("rtd")))
    rta = RT(alloc(nc.semaphore("rta")))
    ld = Sem(alloc(nc.semaphore("ld")))
    ldx = Sem(alloc(nc.semaphore("ldx")))
    st = Sem(alloc(nc.semaphore("st")))
    tns = Sem(alloc(nc.semaphore("tns")))
    dve = Sem(alloc(nc.semaphore("dve")))
    act = Sem(alloc(nc.semaphore("act")))

    def dma(rec, dst_ap, src_ap, sem=ld, waits=(), slow=False):
        return emit(rec, lambda e, d=dst_ap, s=src_ap, sl=slow: e.dma_start(
                        out=d, in_=s, allow_slow_non_contiguous=sl),
                    waits=waits, inc=sem, n=16)

    r = new_recs()
    dma(r["sync"], ident[:, :], id_h.ap())
    dma(r["sync"], bqk8[:, :], gap(bqk_h, 0, 128, 0, [(128, 8)]), slow=True)
    dma(r["sync"], bvrep[:, :], bass.AP(bv_h, 0, [[0, 128], [1, DV]]))
    dma(r["sync"], brow[:, :], gap(bo_h, 0, 1, 0, [(1, D)]))
    dma(r["sync"], gamc[:, :], bass.AP(gam_h, 0, [[0, 128], [1, D]]))
    dma(r["sync"], betc[:, :], bass.AP(bet_h, 0, [[0, 128], [1, D]]))
    p0a_ld = ld.v
    run_block(nc, r)

    r = new_recs()
    def f_consts(e):
        rt = rtd
        rt(e, e.memset(tA[:, :], 6.0))
        rt(e, e.reciprocal(r6[:, :], tA[:, :]))
        rt(e, e.memset(tA[:, :], 100000.0))
        rt(e, e.reciprocal(epsc[:, :], tA[:, :]))
        rt(e, e.memset(tA[:, :], 2.0))
        e.memset(onesr[:, :], 1.0)
        e.memset(pat[:, :, :], 0.0)
        for c in range(4):
            for z in range(2):
                e.memset(pat[z * 64:(z + 1) * 64, c, 2 * c + z:2 * c + z + 1], 1.0)
        e.tensor_scalar(bqs[:, 0:4], bqk8[:, 0:4], 0.125, None, OP.mult)
        return e.tensor_copy(bqs[:, 4:8], bqk8[:, 4:8])
    emit(r["vector"], f_consts, waits=[(ld, ld.v)], inc=dve)

    def f_c2(e):
        rta(e, e.activation(tB[:, :], tA[:, :], AF.Ln))
        return e.activation(cc[:, :], tB[:, :], AF.Exp, scale=-0.5)
    emit(r["scalar"], f_c2, waits=[(dve, dve.v)], inc=act)

    def f_gbc(e):
        e.tensor_scalar(gamc[:, :], gamc[:, :], cc[:, 0:1], None, OP.mult)
        return e.tensor_scalar(betc[:, :], betc[:, :], cc[:, 0:1], None, OP.mult)
    emit(r["vector"], f_gbc, waits=[(act, act.v)], inc=dve)

    for tt in range(NT):
        par = tt % 2
        waits = [(dve, dve.v), (act, act.v)] if tt >= 2 else []
        dma(r["sync"], xb[par][:, :], gap(x_h, 0, 128, tt * 128 * D, [(1, D)]),
            waits=waits, sem=ldx)
        ldv = ldx.v
        emit(r["vector"],
             lambda e, par=par, tt=tt: e.tensor_reduce(
                 sxb[:, tt:tt + 1], xb[par][:, :], AX.X, OP.add),
             waits=[(ldx, ldv)], inc=dve)
        emit(r["scalar"],
             lambda e, par=par, tt=tt: e.activation(
                 dump[:, :], xb[par][:, :], AF.Square, accum_out=sqb[:, tt:tt + 1]),
             waits=[(ldx, ldv)], inc=act)

    def f_stats(e):
        rtd(e, e.tensor_scalar(mu[:, :], sxb[:, :], 1.0 / D, None, OP.mult))
        rtd(e, e.tensor_scalar(sqb[:, :], sqb[:, :], 1.0 / D, None, OP.mult))
        rtd(e, e.tensor_tensor(sxb[:, :], mu[:, :], mu[:, :], OP.mult))
        rtd(e, e.tensor_tensor(sqb[:, :], sqb[:, :], sxb[:, :], OP.subtract))
        return e.tensor_scalar(sqb[:, :], sqb[:, :], epsc[:, 0:1], None, OP.add)
    emit(r["vector"], f_stats, waits=[(dve, dve.v), (act, act.v)], inc=dve)

    def f_rstd0(e):
        rta(e, e.activation(sxb[:, :], sqb[:, :], AF.Ln))
        return e.activation(rstd[:, :], sxb[:, :], AF.Exp, scale=-0.5)
    emit(r["scalar"], f_rstd0, waits=[(dve, dve.v)], inc=act)

    def f_newton(e):
        rtd(e, e.tensor_tensor(sxb[:, :], rstd[:, :], rstd[:, :], OP.mult))
        rtd(e, e.tensor_tensor(sxb[:, :], sqb[:, :], sxb[:, :], OP.mult))
        rtd(e, e.tensor_scalar(sxb[:, :], sxb[:, :], -0.5, 1.5, OP.mult, OP.add))
        return e.tensor_tensor(rstd[:, :], rstd[:, :], sxb[:, :], OP.mult)
    emit(r["vector"], f_newton, waits=[(act, act.v)], inc=dve)
    run_block(nc, r)

    begin_scope()
    wtile0 = scp(nc.sbuf_tensor("wtile0", [128, I_FFN], F32))
    wtile1 = scp(nc.sbuf_tensor("wtile1", [128, I_FFN], F32))
    wq32 = scp(nc.sbuf_tensor("wq32", [128, I_FFN], F32))
    tbuf = scp(nc.sbuf_tensor("tbuf", [128, 22 * 128], F32))
    qw_a = scp(nc.sbuf_tensor("qw_a", [128, I_FFN], F32))
    qw_b1 = scp(nc.sbuf_tensor("qw_b1", [128, I_FFN], F32))
    qw_b2 = scp(nc.sbuf_tensor("qw_b2", [128, I_FFN], F32))
    qw_b3 = scp(nc.sbuf_tensor("qw_b3", [128, I_FFN], F32))
    qw_b4 = scp(nc.sbuf_tensor("qw_b4", [128, I_FFN], F32))
    pT = scp(nc.psum_tensor("pT", [128, 512], F32))
    TMPW = {"a": qw_a, "b1": qw_b1, "b2": qw_b2, "b3": qw_b3, "b4": qw_b4,
            "bm": qt_bm, "sc": qt_sc, "rr": qt_rr}
    r = new_recs()
    jobs = [(wg_h, wqtg_h, I_FFN // 128, D, NBLK_D, I_FFN),
            (wu_h, wqtu_h, I_FFN // 128, D, NBLK_D, I_FFN),
            (wd_h, wdt_h, D // 128, I_FFN, NBLK_I, D)]
    wtl = [wtile0, wtile1]
    ti = 0
    for (src_h, dst_h, ntile, W, nblk, dst_cols) in jobs:
        nch = W // 128
        for rt in range(ntile):
            par = ti % 2
            waits = [(dve, dve.v)] if ti >= 2 else []
            dma(r["sync"], wtl[par][:, 0:W],
                gap(src_h, 0, 128, rt * 128 * W, [(1, W)]), waits=waits)
            ldv = ld.v
            qw = [(ld, ldv)]
            if ti > 0:
                qw.append((tns, tns.v))
            emit_quant(r["vector"], qw, dve, (wtl[par], 0, 0), W, nblk,
                       TMPW, r6[:, 0:1], rtd, q_dst=wq32[:, 0:W])
            dvev = dve.v
            g0 = 0
            while g0 < nch:
                gn = min(4, nch - g0)
                w_ = [(dve, dvev)]
                if ti == 0 and g0 == 0:
                    w_.append((ld, p0a_ld))
                if ti > 0 or g0 > 0:
                    w_.append((act, act.v))
                def f_tr(e, g0=g0, gn=gn):
                    for k in range(gn):
                        c = g0 + k
                        inst = e.transpose(pT[:, k * 128:(k + 1) * 128],
                                           wq32[:, c * 128:(c + 1) * 128],
                                           ident[:, :])
                    return inst
                emit(r["tensor"], f_tr, waits=w_, inc=tns)
                ew = [(tns, tns.v)]
                if g0 == 0:
                    ew.append((st, st.v))
                emit(r["scalar"],
                     lambda e, g0=g0, gn=gn: e.copy(
                         tbuf[:, g0 * 128:(g0 + gn) * 128], pT[:, 0:gn * 128]),
                     waits=ew, inc=act)
                g0 += gn
            emit(r["gpsimd"],
                 lambda e, dst_h=dst_h, nch=nch, rt=rt, dst_cols=dst_cols:
                 e.dma_start(
                     out=gap(dst_h, 0, 128, rt * 128,
                             [(dst_cols * 128, nch), (1, 128)]),
                     in_=gap(tbuf, 0, 128, 0, [(128, nch), (1, 128)])),
                 waits=[(act, act.v)], inc=st, n=16)
            ti += 1
    run_block(nc, r)

    end_scope()
    for b in range(NB):
        bt0 = b * NTS

        begin_scope()
        qk = scp(nc.sbuf_tensor(f"qk_{b}", [128, 8, S], F32))
        vaug = scp(nc.sbuf_tensor(f"vaug_{b}", [128, NTS, NH, 65], F32))
        qm = scp(nc.sbuf_tensor(f"qm_{b}", [128, 2 * NIC], F32))
        m8 = scp(nc.sbuf_tensor(f"m8_{b}", [128, 4], F32))
        mrep = scp(nc.sbuf_tensor(f"mrep_{b}", [128, NH], F32))
        begin_scope()
        Wqk = scp(nc.sbuf_tensor(f"Wqk_{b}", [128, 8, D], F32))
        Wv = scp(nc.sbuf_tensor(f"Wv_{b}", [128, 8, DV], F32))
        xnT = scp(nc.sbuf_tensor(f"xnT_{b}", [128, 8, 512], F32))
        sqs = scp(nc.sbuf_tensor(f"sqs_{b}", [128, 512], F32))
        pQ = scp(nc.psum_tensor(f"pQ_{b}", [128, 512], F32))
        pQ2 = scp(nc.psum_tensor(f"pQ2_{b}", [128, 512], F32))
        pV = scp(nc.psum_tensor(f"pV_{b}", [128, 512], F32))
        pX = scp(nc.psum_tensor(f"pX_{b}", [128, 512], F32))
        pX2 = scp(nc.psum_tensor(f"pX2_{b}", [128, 512], F32))
        pM = scp(nc.psum_tensor(f"pM_{b}", [128, 512], F32))
        r = new_recs()
        dma(r["sync"], Wqk[:, :, :],
            gap(wqk_h, 0, 128, 0, [(128 * D, 8), (1, D)]))
        dma(r["sync"], Wv[:, :, :],
            gap(wv_h, 0, 128, 0, [(128 * DV, 8), (1, DV)]))
        wconst = ld.v
        emit(r["vector"],
             lambda e: e.memset(gap(vaug, 0, 128, 64, [(65, NTS * NH)]), 1.0),
             inc=dve)

        pXl = [pX, pX2]
        pql = [pQ, pQ2]
        for tb in range(S // 512):
            for sub in range(4):
                tt = bt0 + tb * 4 + sub
                par = tt % 2
                w_ = [(dve, dve.v)] if (tb * 4 + sub) >= 2 else []
                dma(r["sync"], xb[par][:, :],
                    gap(x_h, 0, 128, tt * 128 * D, [(1, D)]), waits=w_, sem=ldx)
                ldv = ldx.v
                xw = [(ldx, ldv)]
                if tb * 4 + sub > 0:
                    xw.append((tns, tns.v))
                emit(r["vector"],
                     lambda e, par=par, tt=tt: e.tensor_scalar(
                         xnb[:, :], xb[par][:, :], mu[:, tt:tt + 1],
                         rstd[:, tt:tt + 1], OP.subtract, OP.mult),
                     waits=xw, inc=dve)
                dvev = dve.v
                for g in range(2):
                    w_ = [(dve, dvev)]
                    if tb * 4 + sub > 0 or g > 0:
                        w_.append((act, act.v))
                    def f_tr(e, g=g):
                        for k in range(4):
                            c = g * 4 + k
                            inst = e.transpose(
                                pXl[g][:, k * 128:(k + 1) * 128],
                                xnb[:, c * 128:(c + 1) * 128], ident[:, :])
                        return inst
                    emit(r["tensor"], f_tr, waits=w_, inc=tns)
                    emit(r["scalar"],
                         lambda e, g=g, sub=sub: e.copy(
                             gap(xnT, 0, 128, g * 4 * 512 + sub * 128,
                                 [(512, 4), (1, 128)]),
                             pXl[g][:, 0:512]),
                         waits=[(tns, tns.v)], inc=act)
            actv_xnT = act.v
            for e8 in range(8):
                pq = pql[e8 % 2]
                w_ = [(act, actv_xnT), (ld, wconst)]
                if e8 >= 2 or tb > 0:
                    w_.append((act, act.v))
                def f_m1(e, e8=e8, pq=pq):
                    for c in range(8):
                        inst = e.matmul(pq[:, :],
                                        Wqk[:, c, e8 * 128:(e8 + 1) * 128],
                                        xnT[:, c, 0:512],
                                        start=(c == 0), stop=(c == 7))
                    return inst
                emit(r["tensor"], f_m1, waits=w_, inc=tns)
                sc_ = 0.125 if e8 < 4 else 1.0
                emit(r["scalar"],
                     lambda e, e8=e8, pq=pq, tb=tb, sc_=sc_: e.activation(
                         qk[:, e8, tb * 512:(tb + 1) * 512], pq[:, :],
                         AF.Identity, bias=bqs[:, e8:e8 + 1], scale=sc_),
                     waits=[(tns, tns.v)], inc=act)
            for sub in range(4):
                w_ = [(ld, wconst)]
                if sub > 0 or tb > 0:
                    w_.append((dve, dve.v))
                def f_m2(e, sub=sub):
                    for c in range(8):
                        inst = e.matmul(pV[:, :],
                                        xnT[:, c, sub * 128:(sub + 1) * 128],
                                        Wv[:, c, 0:DV],
                                        start=(c == 0), stop=(c == 7))
                    return inst
                emit(r["tensor"], f_m2, waits=w_, inc=tns)
                jt = tb * 4 + sub
                emit(r["vector"],
                     lambda e, jt=jt: e.tensor_tensor(
                         gap(vaug, 0, 128, jt * NH * 65, [(65, NH), (1, 64)]),
                         pV[:, 0:DV], bvrep[:, :], OP.add),
                     waits=[(tns, tns.v)], inc=dve)

        for half, base in ((0, 0), (1, 4)):
            for ic in range(NIC):
                for c4 in range(4):
                    emit(r["scalar"],
                         lambda e, c4=c4, base=base, ic=ic: e.activation(
                             sqs[:, :],
                             qk[:, base + c4, ic * 512:(ic + 1) * 512],
                             AF.Square),
                         waits=[(tns, tns.v)], inc=act)
                    w_ = [(act, act.v)]
                    if c4 == 0:
                        w_.append((dve, dve.v))
                    emit(r["tensor"],
                         lambda e, c4=c4: e.matmul(
                             pM[0:NH, :], pat[:, c4, :], sqs[:, :],
                             start=(c4 == 0), stop=(c4 == 3)),
                         waits=w_, inc=tns)
                emit(r["vector"],
                     lambda e, half=half, ic=ic: e.tensor_reduce(
                         qm[0:NH, half * NIC + ic:half * NIC + ic + 1],
                         pM[0:NH, :], AX.X, OP.max),
                     waits=[(tns, tns.v)], inc=dve)
        def f_mc0(e):
            rtd(e, e.tensor_reduce(m8[0:NH, 0:1], qm[0:NH, 0:NIC], AX.X, OP.max))
            return e.tensor_reduce(m8[0:NH, 1:2], qm[0:NH, NIC:2 * NIC],
                                   AX.X, OP.max)
        v_mc0 = emit(r["vector"], f_mc0, waits=[(dve, dve.v)], inc=dve)
        emit(r["vector"],
             lambda e: e.tensor_tensor(m8[0:NH, 2:3], m8[0:NH, 0:1],
                                       m8[0:NH, 1:2], OP.mult),
             waits=[(dve, v_mc0)], inc=dve)
        def f_msq(e):
            rta(e, e.activation(m8[0:NH, 3:4], m8[0:NH, 2:3], AF.Ln))
            return e.activation(m8[0:NH, 2:3], m8[0:NH, 3:4], AF.Exp,
                                scale=0.5)
        emit(r["scalar"], f_msq, waits=[(dve, dve.v)], inc=act)
        emit(r["vector"],
             lambda e: e.tensor_scalar(m8[0:NH, 3:4], m8[0:NH, 2:3],
                                       -1.01, None, OP.mult),
             waits=[(act, act.v)], inc=dve)
        for h in range(NH):
            emit(r["gpsimd"],
                 lambda e, h=h: e.dma_start(
                     out=mrep[:, h:h + 1],
                     in_=gap(m8, h, 1, 3, [(0, 128), (0, 1)])),
                 waits=[(dve, dve.v)], inc=st, n=16)
        mrep_done = st.v
        if dbg:
            emit(r["gpsimd"],
                 lambda e, b=b: e.dma_start(out=qkdbg_h[b].ap(), in_=qk[:, :, :]),
                 waits=[(act, act.v)], inc=st, n=16)
            emit(r["gpsimd"],
                 lambda e, b=b: e.dma_start(
                     out=gap(m8dbg_h[b], 0, 128, 0, [(1, 4)]), in_=m8[:, :]),
                 waits=[(st, mrep_done)], inc=st, n=16)
            emit(r["gpsimd"],
                 lambda e, b=b: e.dma_start(
                     out=gap(m8dbg_h[b], 0, 128, 4, [(1, 8)]), in_=mrep[:, :]),
                 waits=[(st, mrep_done)], inc=st, n=16)
        run_block(nc, r)

        end_scope()
        begin_scope()
        PT = scp(nc.sbuf_tensor(f"PT_{b}", [128, NTS, 512], F32))
        rsb = scp(nc.sbuf_tensor(f"rsb_{b}", [128, 512], F32))
        scaler = scp(nc.sbuf_tensor(f"scaler_{b}", [128, 512], F32))
        otmp = scp(nc.sbuf_tensor(f"otmp_{b}", [128, 512], F32))
        pS0 = scp(nc.psum_tensor(f"pS0_{b}", [128, 512], F32))
        pS1 = scp(nc.psum_tensor(f"pS1_{b}", [128, 512], F32))
        pO = scp(nc.psum_tensor(f"pO_{b}", [128, 512], F32))
        r = new_recs()
        pS = [pS0, pS1]
        for h in range(NH):
            zr = (h % 2) * 64
            qc = h // 2
            kc = 4 + h // 2
            tp = (zr, 0)
            for ic in range(NIC):
                actv0 = act.v
                for jc in range(NTS):
                    w_ = []
                    if jc >= 2:
                        w_.append((act, actv0 + jc - 1))
                    elif (h + ic) > 0:
                        w_.append((act, actv0))
                    def f_sim(e, jc=jc, zr=zr, qc=qc, kc=kc, tp=tp, ic=ic):
                        return e.matmul(
                            pS[jc % 2][:, :],
                            gap(qk, zr, 64, kc * S + jc * 128, [(1, 128)]),
                            gap(qk, zr, 64, qc * S + ic * 512, [(1, 512)]),
                            start=True, stop=True, tile_position=tp)
                    emit(r["tensor"], f_sim, waits=w_, inc=tns)
                    ew = [(tns, tns.v)]
                    if h == 0 and ic == 0 and jc == 0:
                        ew.append((st, mrep_done))
                    emit(r["scalar"],
                         lambda e, jc=jc, h=h: e.activation(
                             PT[:, jc, :], pS[jc % 2][:, :], AF.Exp,
                             bias=mrep[:, h:h + 1]),
                         waits=ew, inc=act)
                w_ = [(act, act.v)]
                if h + ic > 0:
                    w_.append((dve, dve.v))
                def f_m4(e, h=h):
                    for jc in range(NTS):
                        inst = e.matmul(
                            pO[0:65, :],
                            gap(vaug, 0, 128, jc * NH * 65 + h * 65,
                                [(1, 65)]),
                            PT[:, jc, :],
                            start=(jc == 0), stop=(jc == NTS - 1))
                    return inst
                emit(r["tensor"], f_m4, waits=w_, inc=tns)
                emit(r["vector"],
                     lambda e: e.reciprocal(rsb[64:65, 0:512],
                                            pO[64:65, 0:512]),
                     waits=[(tns, tns.v)], inc=dve)
                emit(r["sync"],
                     lambda e: e.dma_start(
                         out=scaler[0:64, 0:512],
                         in_=gap(rsb, 64, 1, 0, [(0, 64), (1, 512)])),
                     waits=[(dve, dve.v)], inc=ld, n=16)
                emit(r["vector"],
                     lambda e: e.tensor_tensor(
                         otmp[0:64, :], pO[0:64, :], scaler[0:64, :],
                         OP.mult),
                     waits=[(ld, ld.v), (st, st.v)], inc=dve)
                emit(r["gpsimd"],
                     lambda e, h=h, ic=ic: e.dma_start(
                         out=gap(osc_h, 0, 64,
                                 h * 64 * T + b * S + ic * 512, [(1, 512)]),
                         in_=otmp[0:64, 0:512]),
                     waits=[(dve, dve.v)], inc=st, n=16)
        run_block(nc, r)

        end_scope()
        end_scope()
        begin_scope()
        dummy_shift = scp(nc.sbuf_tensor(f"dummy_{b}", [128, 2048], F32))
        xqT = scp(nc.sbuf_tensor(f"xqT_{b}", [128, 8, S], F32))
        wgb0 = scp(nc.sbuf_tensor(f"wgb0_{b}", [128, 8, 512], F32))
        wgb1 = scp(nc.sbuf_tensor(f"wgb1_{b}", [128, 8, 512], F32))
        wub0 = scp(nc.sbuf_tensor(f"wub0_{b}", [128, 8, 512], F32))
        wub1 = scp(nc.sbuf_tensor(f"wub1_{b}", [128, 8, 512], F32))
        g2buf = scp(nc.sbuf_tensor(f"g2buf_{b}", [128, NTS, 512], I8))
        s2buf = scp(nc.sbuf_tensor(f"s2buf_{b}", [128, NTS, 32], F32))
        egb = scp(nc.sbuf_tensor(f"egb_{b}", [128, 512], F32))
        xq32 = scp(nc.sbuf_tensor(f"xq32_{b}", [128, D], F32))
        q1_a = scp(nc.sbuf_tensor(f"q1_a_{b}", [128, D], F32))
        q1_b1 = scp(nc.sbuf_tensor(f"q1_b1_{b}", [128, D], F32))
        q1_b2 = scp(nc.sbuf_tensor(f"q1_b2_{b}", [128, D], F32))
        q1_b3 = scp(nc.sbuf_tensor(f"q1_b3_{b}", [128, D], F32))
        q1_b4 = scp(nc.sbuf_tensor(f"q1_b4_{b}", [128, D], F32))
        pG = scp(nc.psum_tensor(f"pG_{b}", [128, 512], F32))
        pG2 = scp(nc.psum_tensor(f"pG2_{b}", [128, 512], F32))
        pU = scp(nc.psum_tensor(f"pU_{b}", [128, 512], F32))
        pU2 = scp(nc.psum_tensor(f"pU2_{b}", [128, 512], F32))
        pXa = scp(nc.psum_tensor(f"pXa_{b}", [128, 512], F32))
        pXb = scp(nc.psum_tensor(f"pXb_{b}", [128, 512], F32))
        TMP1 = {"a": q1_a, "b1": q1_b1, "b2": q1_b2, "b3": q1_b3,
                "b4": q1_b4, "bm": qt_bm, "sc": qt_sc, "rr": qt_rr}
        TMPH = {"a": q1_b3, "b1": q1_b1, "b2": q1_b2, "b3": q1_b4,
                "b4": xq32, "bm": qt_bm, "sc": qt_sc, "rr": qt_rr}
        r = new_recs()
        pXl = [pXa, pXb]
        for lt in range(NTS):
            tt = bt0 + lt
            par = tt % 2
            w_ = [(dve, dve.v)] if lt >= 2 else []
            dma(r["sync"], xb[par][:, :],
                gap(x_h, 0, 128, tt * 128 * D, [(1, D)]), waits=w_, sem=ldx)
            ldv = ldx.v
            emit(r["vector"],
                 lambda e, par=par, tt=tt: e.tensor_scalar(
                     xnb[:, :], xb[par][:, :], mu[:, tt:tt + 1],
                     rstd[:, tt:tt + 1], OP.subtract, OP.mult),
                 waits=[(ldx, ldv)], inc=dve)
            qw = [(tns, tns.v)] if lt > 0 else []
            emit_quant(r["vector"], qw, dve, (xnb, 0, 0), D, NBLK_D, TMP1,
                       r6[:, 0:1], rtd, q_dst=xq32[:, :])
            dvev = dve.v
            if dbg:
                emit(r["gpsimd"],
                     lambda e, tt=tt: e.dma_start(
                         out=gap(xqdbg_h, 0, 128, tt * 128 * D, [(1, D)]),
                         in_=xq32[:, :]),
                     waits=[(dve, dvev)], inc=st, n=16)
            for g in range(2):
                w_ = [(dve, dvev)]
                if lt > 0 or g > 0:
                    w_.append((act, act.v))
                def f_tr(e, g=g):
                    for k in range(4):
                        c = g * 4 + k
                        inst = e.transpose(pXl[g][:, k * 128:(k + 1) * 128],
                                           xq32[:, c * 128:(c + 1) * 128],
                                           ident[:, :])
                    return inst
                emit(r["tensor"], f_tr, waits=w_, inc=tns)
                emit(r["scalar"],
                     lambda e, g=g, lt=lt: e.copy(
                         gap(xqT, 0, 128, g * 4 * S + lt * 128,
                             [(S, 4), (1, 128)]),
                         pXl[g][:, 0:512]),
                     waits=[(tns, tns.v)], inc=act)
        actv_xqT = act.v
        wgl = [wgb0, wgb1]
        wul = [wub0, wub1]
        pGl = [pG, pG2]
        pUl = [pU, pU2]
        for ici, (ioff, isz) in enumerate(IC_LIST):
            par = 0
            w_ = [(tns, tns.v), (dve, dve.v)]
            if ici == 0:
                w_.append((st, st.v))
            dma(r["sync"], gap(wgl[par], 0, 128, 0, [(512, 8), (1, isz)]),
                gap(wqtg_h, 0, 128, ioff, [(128 * I_FFN, 8), (1, isz)]),
                waits=w_)
            dma(r["sync"], gap(wul[par], 0, 128, 0, [(512, 8), (1, isz)]),
                gap(wqtu_h, 0, 128, ioff, [(128 * I_FFN, 8), (1, isz)]))
            ldv = ld.v
            nb = isz // 16
            st_prev = st.v
            for lt in range(NTS):
                pp = lt % 2
                w_ = [(ld, ldv), (act, actv_xqT)]
                if ici + lt > 0:
                    w_.append((dve, dve.v))
                def f_gu(e, lt=lt, pp=pp, par=par, isz=isz):
                    for c in range(8):
                        e.matmul(pGl[pp][:, 0:isz],
                                 xqT[:, c, lt * 128:(lt + 1) * 128],
                                 gap(wgl[par], 0, 128, c * 512, [(1, isz)]),
                                 start=(c == 0), stop=(c == 7))
                    for c in range(8):
                        inst = e.matmul(pUl[pp][:, 0:isz],
                                        xqT[:, c, lt * 128:(lt + 1) * 128],
                                        gap(wul[par], 0, 128, c * 512,
                                            [(1, isz)]),
                                        start=(c == 0), stop=(c == 7))
                    return inst
                emit(r["tensor"], f_gu, waits=w_, inc=tns)
                emit(r["scalar"],
                     lambda e, pp=pp, isz=isz: e.activation(
                         egb[:, 0:isz], pGl[pp][:, 0:isz], AF.Exp, scale=-1.0),
                     waits=[(tns, tns.v)], inc=act)
                def f_h(e, pp=pp, isz=isz):
                    rtd(e, e.tensor_scalar(q1_b1[:, 0:isz], egb[:, 0:isz],
                                           1.0, None, OP.add))
                    rtd(e, e.reciprocal(q1_b2[:, 0:isz], q1_b1[:, 0:isz]))
                    rtd(e, e.tensor_tensor(q1_b1[:, 0:isz], pGl[pp][:, 0:isz],
                                           q1_b2[:, 0:isz], OP.mult))
                    return e.tensor_tensor(q1_a[:, 0:isz], q1_b1[:, 0:isz],
                                           pUl[pp][:, 0:isz], OP.mult)
                emit(r["vector"], f_h, waits=[(act, act.v)], inc=dve)
                if dbg:
                    emit(r["gpsimd"],
                         lambda e, lt=lt, ioff=ioff, isz=isz: e.dma_start(
                             out=gap(hdbg_h, 0, 128,
                                     (bt0 + lt) * 128 * I_FFN + ioff, [(1, isz)]),
                             in_=q1_a[:, 0:isz]),
                         waits=[(dve, dve.v)], inc=st, n=16)
                qw = [(st, st_prev)] if lt == 0 else []
                emit_quant(r["vector"], qw, dve, (q1_a, 0, 0), isz, nb,
                           TMPH, r6[:, 0:1], rtd,
                           g2s_dst=g2buf[:, lt, 0:isz],
                           s2n_dst=s2buf[:, lt, 0:nb])
            emit(r["gpsimd"],
                 lambda e, ioff=ioff, isz=isz: e.dma_start(
                     out=gap(g2h_h, 0, 128, bt0 * 128 * I_FFN + ioff,
                             [(128 * I_FFN, NTS), (1, isz)]),
                     in_=gap(g2buf, 0, 128, 0, [(512, NTS), (1, isz)])),
                 waits=[(dve, dve.v)], inc=st, n=16)
            emit(r["gpsimd"],
                 lambda e, ioff=ioff, nb=nb: e.dma_start(
                     out=gap(sch_h, 0, 128, bt0 * 128 * NBLK_I + ioff // 16,
                             [(128 * NBLK_I, NTS), (1, nb)]),
                     in_=gap(s2buf, 0, 128, 0, [(32, NTS), (1, nb)])),
                 waits=[(dve, dve.v)], inc=st, n=16)
        run_block(nc, r)

        end_scope()
        SUPERS = []
        tc0 = 0
        while tc0 < NTS:
            n = min(3, NTS - tc0)
            SUPERS.append((tc0, n))
            tc0 += n
        begin_scope()
        oTn = scp(nc.sbuf_tensor(f"oTn_{b}", [128, 4, S], F32))
        Wo = scp(nc.sbuf_tensor(f"Wo_{b}", [128, 4, D], F32))
        hqsup = scp(nc.sbuf_tensor(f"hqsup_{b}", [128, 22, 3 * 128], F32))
        g2sup = scp(nc.sbuf_tensor(f"g2sup_{b}", [128, 3, I_FFN], I8))
        s2sup = scp(nc.sbuf_tensor(f"s2sup_{b}", [128, 3, NBLK_I], F32))
        wdb0 = scp(nc.sbuf_tensor(f"wdb0_{b}", [128, D], F32))
        wdb1 = scp(nc.sbuf_tensor(f"wdb1_{b}", [128, D], F32))
        rtile = scp(nc.sbuf_tensor(f"rtile_{b}", [128, 128], F32))
        e2b = scp(nc.sbuf_tensor(f"e2b_{b}", [128, D], F32))
        gb = scp(nc.sbuf_tensor(f"gb_{b}", [128, D], F32))
        yb = scp(nc.sbuf_tensor(f"yb_{b}", [128, D], F32))
        sv = scp(nc.sbuf_tensor(f"sv_{b}", [128, 8], F32))
        pD = scp(nc.psum_tensor(f"pD_{b}", [128, 3 * D], F32))
        pT2 = scp(nc.psum_tensor(f"pT2_{b}", [128, 512], F32))
        r = new_recs()
        dma(r["sync"], oTn[:, :, :],
            gap(osc_h, 0, 128, b * S, [(T * 128, 4), (1, S)]),
            waits=[(st, st.v)])
        dma(r["sync"], Wo[:, :, :],
            gap(wo_h, 0, 128, 0, [(128 * D, 4), (1, D)]))
        ld_f2 = ld.v
        wdl = [wdb0, wdb1]
        for (tc0, ntc) in SUPERS:
            w_ = [(dve, dve.v)] if tc0 > 0 else []
            dma(r["sync"], gap(g2sup, 0, 128, 0, [(I_FFN, ntc), (1, I_FFN)]),
                gap(g2h_h, 0, 128, (bt0 + tc0) * 128 * I_FFN,
                    [(128 * I_FFN, ntc), (1, I_FFN)]), waits=w_)
            dma(r["sync"], gap(s2sup, 0, 128, 0, [(NBLK_I, ntc), (1, NBLK_I)]),
                gap(sch_h, 0, 128, (bt0 + tc0) * 128 * NBLK_I,
                    [(128 * NBLK_I, ntc), (1, NBLK_I)]))
            ldv = ld.v
            for stp in range(22):
                for tcx in range(ntc):
                    w_ = [(ld, ldv)]
                    if stp + tcx > 0:
                        w_.append((tns, tns.v))
                    emit(r["vector"],
                         lambda e, stp=stp, tcx=tcx: e.tensor_tensor(
                             rtile[:, 0:128],
                             gap(g2sup, 0, 128, tcx * I_FFN + stp * 128,
                                 [(1, 128)]),
                             gap(s2sup, 0, 128, tcx * NBLK_I + stp * 8,
                                 [(1, 8), (0, 16)]),
                             OP.mult),
                         waits=w_, inc=dve)
                    w_ = [(dve, dve.v)]
                    if stp + tcx > 0 or tc0 > 0:
                        w_.append((act, act.v))
                    emit(r["tensor"],
                         lambda e: e.transpose(pT2[:, 0:128], rtile[:, 0:128],
                                               ident[:, :]),
                         waits=w_, inc=tns)
                    emit(r["scalar"],
                         lambda e, stp=stp, tcx=tcx: e.copy(
                             hqsup[:, stp, tcx * 128:(tcx + 1) * 128],
                             pT2[:, 0:128]),
                         waits=[(tns, tns.v)], inc=act)
            actv_hq = act.v
            for stp in range(22):
                par = stp % 2
                w_ = [(tns, tns.v)] if stp >= 2 else []
                dma(r["sync"], wdl[par][:, :],
                    gap(wdt_h, 0, 128, stp * 128 * D, [(1, D)]), waits=w_)
                ldv = ld.v
                w_ = [(ld, ldv), (act, actv_hq)]
                if stp == 0 and tc0 > 0:
                    w_.append((act, act.v))
                def f_dn(e, stp=stp, par=par, ntc=ntc):
                    for tcx in range(ntc):
                        for hf in range(2):
                            inst = e.matmul(
                                pD[:, tcx * D + hf * 512:
                                   tcx * D + hf * 512 + 512],
                                hqsup[:, stp, tcx * 128:(tcx + 1) * 128],
                                wdl[par][:, hf * 512:(hf + 1) * 512],
                                start=(stp == 0), stop=(stp == 21))
                    return inst
                emit(r["tensor"], f_dn, waits=w_, inc=tns)
            dn_done = tns.v
            for tcx in range(ntc):
                lt = tc0 + tcx
                tt = bt0 + lt
                par = tt % 2
                emit(r["scalar"],
                     lambda e, tcx=tcx: e.activation(
                         e2b[:, :], pD[:, tcx * D:(tcx + 1) * D], AF.Exp,
                         scale=-1.0),
                     waits=[(tns, dn_done), (dve, dve.v)], inc=act)
                def f_g(e, tcx=tcx):
                    rtd(e, e.tensor_scalar(e2b[:, :], e2b[:, :], 1.0, None,
                                           OP.add))
                    rtd(e, e.reciprocal(gb[:, :], e2b[:, :]))
                    return e.tensor_tensor(gb[:, :],
                                           pD[:, tcx * D:(tcx + 1) * D],
                                           gb[:, :], OP.mult)
                emit(r["vector"], f_g, waits=[(act, act.v)], inc=dve)
                dve_g = dve.v
                dma(r["sync"], xb[par][:, :],
                    gap(x_h, 0, 128, tt * 128 * D, [(1, D)]),
                    waits=[(dve, dve_g)], sem=ldx)
                ld_x = ldx.v
                def f_y(e, tcx=tcx, lt=lt):
                    for hf in range(2):
                        for k in range(4):
                            e.matmul(pD[:, tcx * D + hf * 512:
                                         tcx * D + hf * 512 + 512],
                                     oTn[:, k, lt * 128:(lt + 1) * 128],
                                     Wo[:, k, hf * 512:(hf + 1) * 512],
                                     start=(k == 0), stop=False)
                        inst = e.matmul(
                            pD[:, tcx * D + hf * 512: tcx * D + hf * 512 + 512],
                            onesr[:, 0:128], brow[:, hf * 512:(hf + 1) * 512],
                            start=False, stop=True)
                    return inst
                emit(r["tensor"], f_y,
                     waits=[(dve, dve_g), (ld, ld_f2)], inc=tns)
                tns_y = tns.v
                emit(r["vector"],
                     lambda e, tcx=tcx: e.tensor_reduce(
                         sv[:, 0:1], pD[:, tcx * D:tcx * D + 512], AX.X, OP.add),
                     waits=[(tns, tns_y)], inc=dve)
                emit(r["scalar"],
                     lambda e, tcx=tcx: e.activation(
                         dump[:, 0:512], pD[:, tcx * D + 512:(tcx + 1) * D],
                         AF.Square, accum_out=sv[:, 3:4]),
                     waits=[(tns, tns_y)], inc=act)
                emit(r["vector"],
                     lambda e, tcx=tcx: e.tensor_reduce(
                         sv[:, 1:2], pD[:, tcx * D + 512:(tcx + 1) * D],
                         AX.X, OP.add),
                     waits=[(act, act.v)], inc=dve)
                emit(r["scalar"],
                     lambda e, tcx=tcx: e.activation(
                         dump[:, 0:512], pD[:, tcx * D:tcx * D + 512],
                         AF.Square, accum_out=sv[:, 2:3]),
                     waits=[(dve, dve.v)], inc=act)
                def f_v2(e):
                    rtd(e, e.tensor_tensor(sv[:, 0:1], sv[:, 0:1], sv[:, 1:2],
                                           OP.add))
                    rtd(e, e.tensor_tensor(sv[:, 2:3], sv[:, 2:3], sv[:, 3:4],
                                           OP.add))
                    rtd(e, e.tensor_scalar(sv[:, 0:1], sv[:, 0:1], 1.0 / D,
                                           None, OP.mult))
                    rtd(e, e.tensor_scalar(sv[:, 2:3], sv[:, 2:3], 1.0 / D,
                                           None, OP.mult))
                    rtd(e, e.tensor_tensor(sv[:, 1:2], sv[:, 0:1], sv[:, 0:1],
                                           OP.mult))
                    rtd(e, e.tensor_tensor(sv[:, 2:3], sv[:, 2:3], sv[:, 1:2],
                                           OP.subtract))
                    return e.tensor_scalar(sv[:, 2:3], sv[:, 2:3],
                                           epsc[:, 0:1], None, OP.add)
                emit(r["vector"], f_v2,
                     waits=[(act, act.v), (dve, dve.v)], inc=dve)
                def f_r20(e):
                    rta(e, e.activation(sv[:, 4:5], sv[:, 2:3], AF.Ln))
                    return e.activation(sv[:, 5:6], sv[:, 4:5], AF.Exp,
                                        scale=-0.5)
                emit(r["scalar"], f_r20, waits=[(dve, dve.v)], inc=act)
                def f_r2n(e):
                    rtd(e, e.tensor_tensor(sv[:, 6:7], sv[:, 5:6], sv[:, 5:6],
                                           OP.mult))
                    rtd(e, e.tensor_tensor(sv[:, 6:7], sv[:, 2:3], sv[:, 6:7],
                                           OP.mult))
                    rtd(e, e.tensor_scalar(sv[:, 6:7], sv[:, 6:7], -0.5, 1.5,
                                           OP.mult, OP.add))
                    return e.tensor_tensor(sv[:, 5:6], sv[:, 5:6], sv[:, 6:7],
                                           OP.mult)
                emit(r["vector"], f_r2n, waits=[(act, act.v)], inc=dve)
                def f_tail(e, tcx=tcx, tt=tt, par=par):
                    e.tensor_scalar(yb[:, 0:512],
                                    pD[:, tcx * D:tcx * D + 512],
                                    sv[:, 0:1], sv[:, 5:6],
                                    OP.subtract, OP.mult)
                    rtd(e, e.tensor_scalar(yb[:, 512:D],
                                           pD[:, tcx * D + 512:(tcx + 1) * D],
                                           sv[:, 0:1], sv[:, 5:6],
                                           OP.subtract, OP.mult))
                    rtd(e, e.tensor_tensor(yb[:, :], yb[:, :], gamc[:, :],
                                           OP.mult))
                    rtd(e, e.tensor_scalar(xnb[:, :], xb[par][:, :],
                                           mu[:, tt:tt + 1],
                                           rstd[:, tt:tt + 1],
                                           OP.subtract, OP.mult))
                    rtd(e, e.scalar_tensor_tensor(xnb[:, :], xnb[:, :],
                                                  cc[:, 0:1], betc[:, :],
                                                  OP.mult, OP.add))
                    rtd(e, e.tensor_tensor(yb[:, :], yb[:, :], xnb[:, :],
                                           OP.add))
                    rtd(e, e.tensor_tensor(yb[:, :], yb[:, :], gb[:, :],
                                           OP.mult))
                    return e.tensor_tensor(yb[:, :], yb[:, :], xb[par][:, :],
                                           OP.add)
                emit(r["vector"], f_tail,
                     waits=[(dve, dve.v), (ldx, ld_x), (st, st.v)], inc=dve)
                emit(r["gpsimd"],
                     lambda e, tt=tt: e.dma_start(
                         out=gap(out_h, 0, 128, tt * 128 * D, [(1, D)]),
                         in_=yb[:, :]),
                     waits=[(dve, dve.v)], inc=st, n=16)
                emit(r["scalar"],
                     lambda e: e.copy(tB[:, 0:1], tA[:, 0:1]),
                     waits=[(dve, dve.v)], inc=act)
        run_block(nc, r)

        end_scope()
    r = new_recs()
    r["gpsimd"].append(lambda e, v=st.v: e.wait_ge(st.h, v))
    run_block(nc, r)

    for cm in reversed(ctx):
        cm.__exit__(None, None, None)
    return nc


def _host_inputs(x_full, W_qk, b_qk, W_v, b_v, W_out, b_out, gam, bet,
                 w_gate, w_up, w_down, NB, S):
    perm = np.empty(D, dtype=np.int64)
    for h in range(NH):
        for d in range(DH):
            perm[h * DH + d] = h * 2 * DH + 2 * d
            perm[DV + h * DH + d] = h * 2 * DH + 2 * d + 1
    wqk_p = np.ascontiguousarray(np.asarray(W_qk, np.float32)[:, perm])
    bqk_p = np.ascontiguousarray(np.asarray(b_qk, np.float32)[perm])
    ident = np.eye(128, dtype=np.float32)
    shared = {
        "wqk": wqk_p, "bqk": bqk_p,
        "wv": np.ascontiguousarray(np.asarray(W_v, np.float32)),
        "bv": np.ascontiguousarray(np.asarray(b_v, np.float32)),
        "wo": np.ascontiguousarray(np.asarray(W_out, np.float32)),
        "bo": np.ascontiguousarray(np.asarray(b_out, np.float32)),
        "gam": np.ascontiguousarray(np.asarray(gam, np.float32)),
        "bet": np.ascontiguousarray(np.asarray(bet, np.float32)),
        "wg": np.ascontiguousarray(np.asarray(w_gate, np.float32)),
        "wu": np.ascontiguousarray(np.asarray(w_up, np.float32)),
        "wd": np.ascontiguousarray(np.asarray(w_down, np.float32)),
        "ident": ident,
    }
    n_cores = x_full.shape[0] // NB
    maps = []
    for i in range(n_cores):
        xc = np.ascontiguousarray(
            np.asarray(x_full[i * NB:(i + 1) * NB], np.float32).reshape(NB * S, D))
        m = dict(shared)
        m["x"] = xc
        maps.append(m)
    return maps


def kernel(L0_attn_x, L0_attn_W_qk, L0_attn_b_qk, L0_attn_W_v, L0_attn_b_v,
           L0_attn_W_out, L0_attn_b_out, L0_attn_ln_gamma, L0_attn_ln_beta,
           L0_ffn_w_gate, L0_ffn_w_up, L0_ffn_w_down):
    NB, S = 2, 2048
    B = L0_attn_x.shape[0]
    n_cores = B // NB
    maps = _host_inputs(L0_attn_x, L0_attn_W_qk, L0_attn_b_qk, L0_attn_W_v,
                        L0_attn_b_v, L0_attn_W_out, L0_attn_b_out,
                        L0_attn_ln_gamma, L0_attn_ln_beta,
                        L0_ffn_w_gate, L0_ffn_w_up, L0_ffn_w_down, NB, S)
    nc = build(NB=NB, S=S)
    res = run_bass_kernel_spmd(nc, maps, core_ids=list(range(n_cores)))
    outs = [np.asarray(r["out"], np.float32).reshape(NB, S, D)
            for r in res.results]
    return np.ascontiguousarray(np.concatenate(outs, axis=0))
